# version 03
# baseline (speedup 1.0000x reference)
import math
import numpy as np
import ml_dtypes
from contextlib import ExitStack

import concourse.bass as bass
import concourse.mybir as mybir
from concourse.bass_utils import run_bass_kernel_spmd

F32 = mybir.dt.float32
BF16 = mybir.dt.bfloat16
I8 = mybir.dt.int8
I32 = mybir.dt.int32
AF = mybir.ActivationFunctionType
ALU = mybir.AluOpType
AX = mybir.AxisListType

S = 4096
D = 4096
NHL = 16
NKVL = 4
HD = 128
FI = 11008
NIT = 86
NHT = 32
TOK = 2048
SCALE = float(HD) ** -0.5
CNT = float(FI * D)
EPS = 1e-5
LNEPS = 1e-5
NEG = -1.0e30
P1G_ROWS = FI // 8
P1D_ROWS = D // 8

ENGINES = ("tensor", "vector", "scalar", "gpsimd", "sync")


class Plan:
    def __init__(self):
        self.ops = {e: [] for e in ENGINES}
        self.count = {}
        self.seen = {e: {} for e in ENGINES}

    def sem(self, name):
        self.count[name] = 0

    def emit(self, eng, build, waits=(), inc=None, inc_by=1):
        w = []
        for (s, v) in waits:
            if v is None or v <= 0:
                continue
            if self.seen[eng].get(s, 0) >= v:
                continue
            self.seen[eng][s] = v
            w.append((s, v))
        val = None
        if inc is not None:
            self.count[inc] += inc_by
            val = self.count[inc]
        self.ops[eng].append((w, build, inc, inc_by))
        return val

    def barrier(self):
        snap = dict(self.count)
        for e in ENGINES:
            for s, v in snap.items():
                if v > 0:
                    self.emit(e, None, waits=((s, v),))

    def drain(self, block, sems):
        for eng in ENGINES:
            lst = self.ops[eng]
            if not lst:
                continue
            dec = getattr(block, eng)

            def section(e, lst=lst):
                for (w, build, inc, inc_by) in lst:
                    for (s, v) in w:
                        e.wait_ge(sems[s], v)
                    if build is None:
                        continue
                    ins = build(e)
                    if inc is not None:
                        ins.then_inc(sems[inc], inc_by)

            dec(section)
        self.ops = {e: [] for e in ENGINES}


def build_program(stop_after=None, debug=()):
    nc = bass.Bass("TRN2", target_bir_lowering=False)

    x_full = nc.dram_tensor("x_full", [S, D], F32, kind="ExternalInput").ap()
    x_own = nc.dram_tensor("x_own", [TOK, D], F32, kind="ExternalInput").ap()
    wq = nc.dram_tensor("wq", [D, NHL * HD], F32, kind="ExternalInput").ap()
    wk = nc.dram_tensor("wk", [D, NKVL * HD], F32, kind="ExternalInput").ap()
    wv = nc.dram_tensor("wv", [D, NKVL * HD], F32, kind="ExternalInput").ap()
    wo = nc.dram_tensor("wo", [D, D], F32, kind="ExternalInput").ap()
    wg = nc.dram_tensor("wg", [FI, D], F32, kind="ExternalInput").ap()
    wu = nc.dram_tensor("wu", [FI, D], F32, kind="ExternalInput").ap()
    wd = nc.dram_tensor("wd", [D, FI], F32, kind="ExternalInput").ap()
    tokbase = nc.dram_tensor("tokbase", [1, 1], I32, kind="ExternalInput").ap()
    mask_tri = nc.dram_tensor("mask_tri", [128, 128], F32, kind="ExternalInput").ap()
    identity = nc.dram_tensor("identity", [128, 128], BF16, kind="ExternalInput").ap()

    out = nc.dram_tensor("out", [TOK, D], F32, kind="ExternalOutput").ap()

    xbf = nc.dram_tensor("xbf", [S, D], BF16)
    wqb = nc.dram_tensor("wqb", [NHL, 128, NHT * 128], BF16)
    wo_bf = nc.dram_tensor("wo_bf", [8, 128, NHT * 512], BF16)
    qT = nc.dram_tensor("qT", [NHL, 128, S], BF16)
    oav_in = nc.dram_tensor("oav_in", [NHL * 128, S], BF16)
    oav_ag = nc.dram_tensor("oav_ag", [2 * NHL * 128, S], BF16)
    x1d = nc.dram_tensor("x1d", [TOK, D], BF16)
    qg_t = nc.dram_tensor("qg_t", [NIT, 128, NHT * 128], BF16)
    qu_t = nc.dram_tensor("qu_t", [NIT, 128, NHT * 128], BF16)
    qd_t = nc.dram_tensor("qd_t", [NIT, 128, D], BF16)
    attn_d = nc.dram_tensor("attn_d", [TOK, D], F32)
    lnv_d = nc.dram_tensor("lnv_d", [TOK, 8], F32)
    scl_d = nc.dram_tensor("scl_d", [1, 8], F32)
    den_d = nc.dram_tensor("den_d", [8, 512], F32)
    avf_d = nc.dram_tensor("avf_d", [8 * 128, 512], F32)
    exp_d = nc.dram_tensor("exp_d", [8 * 128, 512], BF16)
    sco_d = nc.dram_tensor("sco_d", [8 * 128, 512], F32)
    scratch = dict(xbf=xbf, wqb=wqb, wo_bf=wo_bf, qT=qT, oav_in=oav_in, oav_ag=oav_ag,
                   x1d=x1d, qg_t=qg_t, qu_t=qu_t, qd_t=qd_t, attn_d=attn_d,
                   lnv_d=lnv_d, den_d=den_d, avf_d=avf_d, exp_d=exp_d, sco_d=sco_d,
                   scl_d=scl_d)

    dbg = {}
    for name in debug:
        t = scratch[name]
        shp = [int(np.prod(t.shape[:-1])), int(t.shape[-1])]
        dbg[name] = nc.dram_tensor("dbg_" + name, shp, t.dtype, kind="ExternalOutput").ap()

    plan = Plan()
    for s in ("sp", "sa", "sv", "sl", "sg", "scc", "sout"):
        plan.sem(s)

    ctx = ExitStack()
    sems = {}
    with ctx:
        for s in ("sp", "sa", "sv", "sl", "sg", "scc", "sout"):
            sems[s] = ctx.enter_context(nc.semaphore(s))
        block = ctx.enter_context(nc.Block())

        mask_sb = ctx.enter_context(nc.sbuf_tensor("mask_sb", [128, 128], F32))
        id_sb = ctx.enter_context(nc.sbuf_tensor("id_sb", [128, 128], BF16))
        ones_bk = ctx.enter_context(nc.sbuf_tensor("ones_bk", [128, 1], BF16))
        ones_fr = ctx.enter_context(nc.sbuf_tensor("ones_fr", [1, 128], F32))
        ones_fc = ctx.enter_context(nc.sbuf_tensor("ones_fc", [128, 1], F32))
        scl_sb = ctx.enter_context(nc.sbuf_tensor("scl_sb", [128, 4], F32))
        tb_sb = ctx.enter_context(nc.sbuf_tensor("tb_sb", [1, 1], I32))
        abctx = ExitStack()
        K_sb = abctx.enter_context(nc.sbuf_tensor("K_sb", [128, NKVL * S], BF16))
        V_sb = abctx.enter_context(nc.sbuf_tensor("V_sb", [128, 32 * NKVL * 128], BF16))

        with ExitStack() as actx:
            e = actx.enter_context
            xT = [e(nc.sbuf_tensor("xT0", [128, NHT * 512], BF16))] * 2
            wk_sb = e(nc.sbuf_tensor("wk_sb", [128, NHT * 512], BF16))
            wv_sb = e(nc.sbuf_tensor("wv_sb", [128, NHT * 512], BF16))
            wq_sb = [e(nc.sbuf_tensor(f"wq_sb{i}", [128, NHT * 128], BF16)) for i in range(2)]
            qstage = [e(nc.sbuf_tensor(f"qstage{i}", [128, 512], BF16)) for i in range(2)]
            p1buf = [e(nc.sbuf_tensor(f"p1buf{i}", [128, 2048], F32)) for i in range(2)]
            p1tmp = e(nc.sbuf_tensor("p1tmp", [128, 1], F32))
            p1t16 = e(nc.sbuf_tensor("p1t16", [128, 16], F32))
            p1acc = e(nc.sbuf_tensor("p1acc", [128, 3], F32))
            kcomp = e(nc.sbuf_tensor("kcomp", [128, 3], F32))
            kah_y = e(nc.sbuf_tensor("kah_y", [128, 1], F32))
            kah_t = e(nc.sbuf_tensor("kah_t", [128, 1], F32))
            arstage = e(nc.sbuf_tensor("arstage", [1, 8], F32))
            bsrc = e(nc.sbuf_tensor("bsrc", [1, 4], F32))
            ps_k = [e(nc.psum_tensor(f"ps_k{i}", [128, 512], F32)) for i in range(2)]
            ps_v = [e(nc.psum_tensor(f"ps_v{i}", [128, 512], F32)) for i in range(2)]
            ps_q = [e(nc.psum_tensor(f"ps_q{i}", [128, 512], F32)) for i in range(2)]
            ps_p1 = e(nc.psum_tensor("ps_p1", [1, 8], F32))
            ps_b4 = e(nc.psum_tensor("ps_b4", [128, 4], F32))

            plan.emit("sync", lambda s: s.dma_start(mask_sb[:], mask_tri), inc="sl", inc_by=16)
            plan.emit("sync", lambda s: s.dma_start(id_sb[:], identity), inc="sl", inc_by=16)
            tb_ld = plan.emit("sync", lambda s: s.dma_start(tb_sb[:], tokbase), inc="sl", inc_by=16)
            plan.emit("vector", lambda v: v.memset(ones_bk[:], 1.0), inc="sv")
            plan.emit("vector", lambda v: v.memset(ones_fr[:], 1.0), inc="sv")
            plan.emit("vector", lambda v: v.memset(ones_fc[:], 1.0), inc="sv")
            plan.emit("vector", lambda v: v.memset(p1acc[:], 0.0), inc="sv")
            plan.emit("vector", lambda v: v.memset(kcomp[:], 0.0), inc="sv")
            plan.emit("vector", lambda v: v.memset(arstage[:], 0.0), inc="sv")
            memset_done = plan.count["sv"]
            def _load_tab(sc):
                return sc.add_instruction(mybir.InstLoadActFuncSet(
                    name=nc.get_next_instruction_name(), act_func_set_id=6, ins=[], outs=[]))
            plan.emit("scalar", _load_tab, inc="sa")

            xbf_done = []
            for i in range(4):
                plan.emit("gpsimd",
                          lambda g, i=i: g.dma_start(xbf.ap()[i * 1024:(i + 1) * 1024, :],
                                                     x_full[i * 1024:(i + 1) * 1024, :]),
                          waits=(("sl", tb_ld),), inc="sg", inc_by=16)
                xbf_done.append(plan.count["sg"])
            wqb_done = []
            for h in range(NHL):
                for piece in range(2):
                    def mk(g, h=h, piece=piece):
                        src = bass.AP(wq.tensor, piece * 16 * 128 * (NHL * HD) + h * HD,
                                      [[NHL * HD, 128], [128 * NHL * HD, 16], [1, 128]])
                        dst = bass.AP(wqb, h * 128 * NHT * 128 + piece * 16 * 128,
                                      [[NHT * 128, 128], [128, 16], [1, 128]])
                        return g.dma_start(dst, src)
                    plan.emit("gpsimd", mk, inc="sg", inc_by=16)
                wqb_done.append(plan.count["sg"])
            for db in range(8):
                for piece in range(2):
                    def mk(g, db=db, piece=piece):
                        src = bass.AP(wo.tensor, piece * 16 * 128 * D + db * 512,
                                      [[D, 128], [128 * D, 16], [1, 512]])
                        dst = bass.AP(wo_bf, db * 128 * NHT * 512 + piece * 16 * 512,
                                      [[NHT * 512, 128], [512, 16], [1, 512]])
                        return g.dma_start(dst, src)
                    plan.emit("gpsimd", mk, inc="sg", inc_by=16)
            wobf_done = plan.count["sg"]
            for (dstt, srct) in ((wk_sb, wk), (wv_sb, wv)):
                for piece in range(2):
                    def mk(g, dstt=dstt, srct=srct, piece=piece):
                        src = bass.AP(srct.tensor, piece * 16 * 128 * (NKVL * HD),
                                      [[NKVL * HD, 128], [128 * NKVL * HD, 16], [1, 512]])
                        dst = dstt[:, piece * 16 * 512:(piece + 1) * 16 * 512]
                        return g.dma_start(dst, src)
                    plan.emit("gpsimd", mk, inc="sg", inc_by=16)
            wkv_done = plan.count["sg"]

            xt_free = 0
            xt_ready = [0, 0]
            k_free = [0, 0]
            v_free = [0, 0]
            q_free = [0, 0]
            qs_free = [0, 0]
            wq_ready = [0, 0]
            wq_free = [0, 0]
            for hc in range(8):
                par = hc % 2
                for d in range(NHT):
                    xt_ready[par] = plan.emit(
                        "sync",
                        lambda s, d=d, hc=hc, par=par: s.dma_start(
                            xT[par][:, d * 512:(d + 1) * 512],
                            xbf.ap()[hc * 512:(hc + 1) * 512, d * 128:(d + 1) * 128],
                            transpose=True),
                        waits=(("sp", xt_free), ("sg", xbf_done[(hc * 512) // 1024])),
                        inc="sl", inc_by=16)
                for kvh in range(NKVL):
                    kpar = kvh % 2
                    for d in range(NHT):
                        wts = [("sl", xt_ready[par]), ("sg", wkv_done)]
                        if d == 0:
                            wts.append(("sv", k_free[kpar]))
                        mm = plan.emit(
                            "tensor",
                            lambda t, kvh=kvh, d=d, kpar=kpar, par=par:
                            t.matmul(ps_k[kpar][:],
                                     wk_sb[:, d * 512 + kvh * 128: d * 512 + (kvh + 1) * 128],
                                     xT[par][:, d * 512:(d + 1) * 512],
                                     start=(d == 0), stop=(d == NHT - 1)),
                            waits=wts, inc="sp")
                    k_free[kpar] = plan.emit(
                        "vector",
                        lambda v, kvh=kvh, hc=hc, kpar=kpar:
                        v.tensor_copy(K_sb[:, kvh * S + hc * 512: kvh * S + (hc + 1) * 512],
                                      ps_k[kpar][:]),
                        waits=(("sp", mm),), inc="sv")
                for tt in range(4):
                    vpar = tt % 2
                    for d in range(NHT):
                        wts = [("sl", xt_ready[par])]
                        if d == 0:
                            wts.append(("sv", v_free[vpar]))
                        mm = plan.emit(
                            "tensor",
                            lambda t, tt=tt, d=d, vpar=vpar, par=par:
                            t.matmul(ps_v[vpar][:],
                                     xT[par][:, d * 512 + tt * 128: d * 512 + (tt + 1) * 128],
                                     wv_sb[:, d * 512:(d + 1) * 512],
                                     start=(d == 0), stop=(d == NHT - 1)),
                            waits=wts, inc="sp")
                    v_free[vpar] = plan.emit(
                        "vector",
                        lambda v, tt=tt, hc=hc, vpar=vpar:
                        v.tensor_copy(V_sb[:, (hc * 4 + tt) * 512:(hc * 4 + tt + 1) * 512],
                                      ps_v[vpar][:]),
                        waits=(("sp", mm),), inc="sv")
                for h in range(NHL):
                    wpar = h % 2
                    wq_ready[wpar] = plan.emit(
                        "sync",
                        lambda s, h=h, wpar=wpar: s.dma_start(
                            wq_sb[wpar][:],
                            bass.AP(wqb, h * 128 * NHT * 128, [[NHT * 128, 128], [1, NHT * 128]])),
                        waits=(("sp", wq_free[wpar]), ("sg", wqb_done[h])),
                        inc="sl", inc_by=16)
                    qpar = h % 2
                    for d in range(NHT):
                        wts = [("sl", xt_ready[par]), ("sl", wq_ready[wpar])]
                        if d == 0:
                            wts.append(("sa", q_free[qpar]))
                        mm = plan.emit(
                            "tensor",
                            lambda t, d=d, qpar=qpar, wpar=wpar, par=par:
                            t.matmul(ps_q[qpar][:],
                                     wq_sb[wpar][:, d * 128:(d + 1) * 128],
                                     xT[par][:, d * 512:(d + 1) * 512],
                                     start=(d == 0), stop=(d == NHT - 1)),
                            waits=wts, inc="sp")
                    wq_free[wpar] = mm
                    ev = plan.emit(
                        "scalar",
                        lambda sc, qpar=qpar: sc.activation(qstage[qpar][:], ps_q[qpar][:], AF.Copy),
                        waits=(("sp", mm), ("sg", qs_free[qpar])), inc="sa")
                    q_free[qpar] = ev
                    qs_free[qpar] = plan.emit(
                        "gpsimd",
                        lambda g, h=h, hc=hc, qpar=qpar: g.dma_start(
                            bass.AP(qT, h * 128 * S + hc * 512, [[S, 128], [1, 512]]),
                            qstage[qpar][:]),
                        waits=(("sa", ev),), inc="sg", inc_by=16)
                xt_free = plan.count["sp"]

            p1_specs = []
            for i in range(NHT):
                p1_specs.append((2, wd, i * 128, 128, 10240, 768))
                for c0 in (8192, 6144, 4096, 2048, 0):
                    p1_specs.append((2, wd, i * 128, 128, c0, 2048))
            for m in range(2):
                t = (wg, wu)[m]
                for i in range(NIT):
                    p1_specs.append((m, t, i * 128, 128, 2048, 2048))
                    p1_specs.append((m, t, i * 128, 128, 0, 2048))
            acc_last = 0
            red_hist = []
            for idx, (m, t, r0, rows, c0, wdt) in enumerate(p1_specs):
                par = idx % 2
                wts = [("sv", memset_done)]
                if idx >= 2:
                    wts.append(("sv", red_hist[idx - 2]))
                ld = plan.emit("gpsimd",
                               lambda s, t=t, r0=r0, rows=rows, c0=c0, wdt=wdt, par=par:
                               s.dma_start(p1buf[par][0:rows, 0:wdt], t[r0:r0 + rows, c0:c0 + wdt]),
                               waits=wts, inc="sg", inc_by=16)
                rv1 = plan.emit("vector",
                                lambda v, wdt=wdt, par=par:
                                v.tensor_reduce(p1t16[:, 0:wdt // 128],
                                                p1buf[par][:, 0:wdt].rearrange(
                                                    "p (a c) -> p a c", c=128),
                                                AX.X, ALU.add, apply_absolute_value=True),
                                waits=(("sg", ld), ("sv", acc_last)), inc="sv")
                rv = plan.emit("vector",
                               lambda v, wdt=wdt:
                               v.tensor_reduce(p1tmp[:], p1t16[:, 0:wdt // 128],
                                               AX.X, ALU.add),
                               waits=(("sv", rv1),), inc="sv")
                k1 = plan.emit("vector",
                               lambda v, m=m: v.tensor_sub(kah_y[:], p1tmp[:], kcomp[:, m:m + 1]),
                               waits=(("sv", rv),), inc="sv")
                k2 = plan.emit("vector",
                               lambda v, m=m: v.tensor_add(kah_t[:], p1acc[:, m:m + 1], kah_y[:]),
                               waits=(("sv", k1),), inc="sv")
                k3 = plan.emit("vector",
                               lambda v, m=m: v.tensor_sub(kcomp[:, m:m + 1], kah_t[:],
                                                           p1acc[:, m:m + 1]),
                               waits=(("sv", k2),), inc="sv")
                k4 = plan.emit("vector",
                               lambda v, m=m: v.tensor_sub(kcomp[:, m:m + 1], kcomp[:, m:m + 1],
                                                           kah_y[:]),
                               waits=(("sv", k3),), inc="sv")
                acc_last = plan.emit("vector",
                                     lambda v, m=m: v.tensor_copy(p1acc[:, m:m + 1], kah_t[:]),
                                     waits=(("sv", k4),), inc="sv")
                red_hist.append(acc_last)
            for m in range(3):
                plan.emit("tensor",
                          lambda t, m=m: t.matmul(ps_p1[0:1, m:m + 1], p1acc[:, m:m + 1],
                                                  ones_fc[:], start=True, stop=True),
                          waits=(("sv", acc_last),), inc="sp")
            pe_p1 = plan.count["sp"]
            a_evac = plan.emit("scalar",
                               lambda sc: sc.activation(arstage[0:1, 0:3], ps_p1[0:1, 0:3], AF.Copy),
                               waits=(("sp", pe_p1), ("sv", memset_done)), inc="sa")
            cv = plan.emit("vector",
                           lambda v: v.tensor_scalar(arstage[0:1, 4:7], arstage[0:1, 0:3],
                                                     1.0 / 128.0, None, ALU.mult),
                           waits=(("sa", a_evac),), inc="sv")
            cb = plan.emit("tensor",
                           lambda t: t.matmul(ps_b4[:, 0:3], ones_fr[:], arstage[0:1, 4:7],
                                              start=True, stop=True),
                           waits=(("sv", cv),), inc="sp")
            cs = plan.emit("vector",
                           lambda v: v.tensor_tensor(p1acc[:, 0:3], p1acc[:, 0:3],
                                                     ps_b4[:, 0:3], ALU.subtract),
                           waits=(("sp", cb),), inc="sv")
            for m in range(3):
                plan.emit("tensor",
                          lambda t, m=m: t.matmul(ps_p1[0:1, 4 + m:5 + m], p1acc[:, m:m + 1],
                                                  ones_fc[:], start=True, stop=True),
                          waits=(("sv", cs),), inc="sp")
            pe_p2 = plan.count["sp"]
            a_evac2 = plan.emit("scalar",
                                lambda sc: sc.activation(arstage[0:1, 4:7], ps_p1[0:1, 4:7],
                                                         AF.Copy),
                                waits=(("sp", pe_p2),), inc="sa")
            t1 = plan.emit("vector",
                           lambda v: v.tensor_add(arstage[0:1, 0:3], arstage[0:1, 0:3],
                                                  arstage[0:1, 4:7]),
                           waits=(("sa", a_evac2),), inc="sv")
            v1 = plan.emit("vector",
                           lambda v: v.tensor_scalar(arstage[0:1, 0:3], arstage[0:1, 0:3],
                                                     1.0 / CNT, None, ALU.mult),
                           waits=(("sv", t1),), inc="sv")
            v2 = plan.emit("vector",
                           lambda v: v.tensor_scalar(arstage[0:1, 4:7], arstage[0:1, 0:3],
                                                     EPS, None, ALU.add),
                           waits=(("sv", v1),), inc="sv")
            v3 = plan.emit("vector", lambda v: v.reciprocal(bsrc[0:1, 0:3], arstage[0:1, 4:7]),
                           waits=(("sv", v2),), inc="sv")
            v4 = plan.emit("vector",
                           lambda v: v.tensor_tensor(bsrc[0:1, 3:4], arstage[0:1, 0:1],
                                                     arstage[0:1, 1:2], ALU.mult),
                           waits=(("sv", v3),), inc="sv")
            v5 = plan.emit("vector",
                           lambda v: v.tensor_tensor(bsrc[0:1, 3:4], bsrc[0:1, 3:4],
                                                     arstage[0:1, 2:3], ALU.mult),
                           waits=(("sv", v4),), inc="sv")
            bc = plan.emit("tensor",
                           lambda t: t.matmul(ps_b4[:], ones_fr[:], bsrc[0:1, :],
                                              start=True, stop=True),
                           waits=(("sv", v5),), inc="sp")
            scl_ready = plan.emit("vector", lambda v: v.tensor_copy(scl_sb[:], ps_b4[:]),
                                  waits=(("sp", bc),), inc="sv")
            if "scl_d" in dbg:
                dmp = plan.emit("vector",
                                lambda v: v.tensor_copy(arstage[0:1, 4:8], bsrc[0:1, :]),
                                waits=(("sv", scl_ready),), inc="sv")
                plan.emit("gpsimd", lambda g: g.dma_start(scl_d.ap(), arstage[0:1, :]),
                          waits=(("sv", dmp),), inc="sg", inc_by=16)


            plan.emit("sync", lambda s: s.dma_start(mask_sb[:], mask_tri), inc="sl", inc_by=16)
            plan.emit("sync", lambda s: s.dma_start(id_sb[:], identity), inc="sl", inc_by=16)
            plan.emit("sync", lambda s: s.dma_start(tb_sb[:], tokbase), inc="sl", inc_by=16)

            plan.barrier()
            plan.drain(block, sems)

        if stop_after == "A":
            abctx.close()
            _finish_debug(nc, plan, block, sems, dbg, scratch)
            return nc

        with ExitStack() as bctx:
            e = bctx.enter_context
            qh_sb = [e(nc.sbuf_tensor(f"qh{i}", [128, S], BF16)) for i in range(2)]
            exp_sb = [e(nc.sbuf_tensor(f"exp{i}", [128, 512], BF16)) for i in range(2)]
            avf_sb = e(nc.sbuf_tensor("avf", [128, 512], F32))
            rec_sb = e(nc.sbuf_tensor("rec", [1, 512], F32))
            oav_sb = [e(nc.sbuf_tensor(f"oav{i}", [128, S], BF16)) for i in range(2)]
            ps_s = [e(nc.psum_tensor(f"ps_s{i}", [128, 512], F32)) for i in range(2)]
            ps_av = e(nc.psum_tensor("ps_av", [128, 512], F32))
            ps_den = e(nc.psum_tensor("ps_den", [1, 512], F32))
            ps_bc = e(nc.psum_tensor("ps_bc", [128, 512], F32))

            qh_ready = [0, 0]
            qh_free = [0, 0]
            expdump_sg = [0, 0]
            exp_done = [0, 0]
            expbuf_free = [0, 0]
            oav_free = [0, 0]
            av_free = 0
            den_free = 0
            bc_free = 0
            avstage_free = 0

            qh_ready[0] = plan.emit(
                "sync",
                lambda s: s.dma_start(qh_sb[0][:], bass.AP(qT, 0, [[S, 128], [1, S]])),
                inc="sl", inc_by=16)

            for h in range(NHL):
                hpar = h % 2
                kvh = h // 4
                if h + 1 < NHL:
                    qh_ready[1 - hpar] = plan.emit(
                        "sync",
                        lambda s, h=h, p=1 - hpar: s.dma_start(
                            qh_sb[p][:], bass.AP(qT, (h + 1) * 128 * S, [[S, 128], [1, S]])),
                        waits=(("sp", qh_free[1 - hpar]),), inc="sl", inc_by=16)
                for qb in range(8):
                    qs = qb * 512
                    nkt = qb * 4 + 4
                    for kt in range(nkt):
                        par = kt % 2
                        c0 = max(0, kt * 128 - qs)
                        boundary = kt >= qb * 4
                        mm_s = plan.emit(
                            "tensor",
                            lambda t, kvh=kvh, kt=kt, c0=c0, par=par, hpar=hpar, qs=qs:
                            t.matmul(ps_s[par][:, c0:512],
                                     K_sb[:, kvh * S + kt * 128: kvh * S + (kt + 1) * 128],
                                     qh_sb[hpar][:, qs + c0: qs + 512],
                                     start=True, stop=True),
                            waits=(("sa", exp_done[par]), ("sl", qh_ready[hpar])), inc="sp")
                        if boundary:
                            mk = plan.emit(
                                "vector",
                                lambda v, par=par, c0=c0:
                                v.tensor_tensor(ps_s[par][:, c0:c0 + 128],
                                                ps_s[par][:, c0:c0 + 128],
                                                mask_sb[:], ALU.add),
                                waits=(("sp", mm_s),), inc="sv")
                            ewts = [("sv", mk), ("sp", expbuf_free[par])]
                        else:
                            ewts = [("sp", mm_s), ("sp", expbuf_free[par])]
                        ewts.append(("sg", expdump_sg[par]))
                        exp_done[par] = plan.emit(
                            "scalar",
                            lambda sc, par=par, c0=c0:
                            sc.activation(exp_sb[par][:, c0:512], ps_s[par][:, c0:512],
                                          AF.Exp, scale=SCALE),
                            waits=ewts, inc="sa")
                        if h == NHL - 1 and qb < 2 and "exp_d" in dbg:
                            expdump_sg[par] = plan.emit(
                                "gpsimd",
                                lambda g, row=qb * 4 + (kt % 4), par=par: g.dma_start(
                                    exp_d.ap()[row * 128:(row + 1) * 128, :],
                                    exp_sb[par][:]),
                                waits=(("sa", exp_done[par]),), inc="sg", inc_by=16)
                        wts_av = [("sa", exp_done[par])]
                        if kt == 0:
                            wts_av += [("sa", av_free), ("sv", den_free)]
                        plan.emit(
                            "tensor",
                            lambda t, kt=kt, kvh=kvh, c0=c0, par=par, nkt=nkt:
                            t.matmul(ps_av[:, c0:512],
                                     V_sb[:, kt * 512 + kvh * 128: kt * 512 + (kvh + 1) * 128],
                                     exp_sb[par][:, c0:512],
                                     start=(kt == 0), stop=(kt == nkt - 1)),
                            waits=wts_av, inc="sp")
                        expbuf_free[par] = plan.emit(
                            "tensor",
                            lambda t, kt=kt, c0=c0, par=par, nkt=nkt:
                            t.matmul(ps_den[0:1, c0:512], ones_bk[:], exp_sb[par][:, c0:512],
                                     start=(kt == 0), stop=(kt == nkt - 1)),
                            inc="sp")
                    pe_done = plan.count["sp"]
                    den_free = plan.emit(
                        "vector", lambda v: v.reciprocal(rec_sb[:], ps_den[:]),
                        waits=(("sp", pe_done),), inc="sv")
                    bc_mm = plan.emit(
                        "tensor",
                        lambda t: t.matmul(ps_bc[:], ones_fr[:], rec_sb[:], start=True, stop=True),
                        waits=(("sv", den_free), ("sv", bc_free)), inc="sp")
                    av_free = plan.emit(
                        "scalar", lambda sc: sc.activation(avf_sb[:], ps_av[:], AF.Copy),
                        waits=(("sp", pe_done), ("sv", avstage_free)), inc="sa")
                    if h == NHL - 1 and "den_d" in dbg:
                        plan.emit("vector",
                                  lambda v, qb=qb: v.tensor_copy(rec_sb[:], ps_den[:]),
                                  waits=(("sv", den_free),), inc="sv")
                        plan.emit("gpsimd",
                                  lambda g, qb=qb: g.dma_start(
                                      den_d.ap()[qb:qb + 1, :], rec_sb[:]),
                                  waits=(("sv", plan.count["sv"]),), inc="sg", inc_by=16)
                        plan.emit("gpsimd",
                                  lambda g, qb=qb: g.dma_start(
                                      avf_d.ap()[qb * 128:(qb + 1) * 128, :], avf_sb[:]),
                                  waits=(("sa", av_free),), inc="sg", inc_by=16)
                    bc_free = plan.emit(
                        "vector",
                        lambda v, qb=qb, hpar=hpar:
                        v.tensor_tensor(oav_sb[hpar][:, qb * 512:(qb + 1) * 512],
                                        avf_sb[:], ps_bc[:], ALU.mult),
                        waits=(("sa", av_free), ("sp", bc_mm), ("sg", oav_free[hpar]),
                               ("sg", plan.count["sg"])),
                        inc="sv")
                    avstage_free = bc_free
                qh_free[hpar] = plan.count["sp"]
                oav_free[hpar] = plan.emit(
                    "gpsimd",
                    lambda g, h=h, hpar=hpar: g.dma_start(
                        bass.AP(oav_in, h * 128 * S, [[S, 128], [1, S]]), oav_sb[hpar][:]),
                    waits=(("sv", bc_free),), inc="sg", inc_by=16)

            plan.emit(
                "gpsimd",
                lambda g: g.collective_compute(
                    "AllGather", ALU.bypass,
                    replica_groups=[[0, 1], [2, 3], [4, 5], [6, 7]],
                    ins=[oav_in.ap().opt()], outs=[oav_ag.ap().opt()]),
                waits=(("sg", plan.count["sg"]),), inc="scc")

            plan.barrier()
            plan.drain(block, sems)

        abctx.close()
        if stop_after == "B":
            _finish_debug(nc, plan, block, sems, dbg, scratch)
            return nc

        prep_specs = []
        for mat in (0, 1):
            for it in range(NIT):
                for ch in range(2):
                    prep_specs.append((mat, it, ch * 2048, 2048))
        for ht in range(NHT):
            for (c0, wdt) in ((0, 2048), (2048, 2048), (4096, 2048), (6144, 2048),
                              (8192, 2048), (10240, 768)):
                prep_specs.append((2, ht, c0, wdt))
        n_prep = len(prep_specs)
        prep_slices = [prep_specs[(n_prep * i) // 4:(n_prep * (i + 1)) // 4] for i in range(4)]

        with ExitStack() as cctx:
            e = cctx.enter_context
            oag_sb = e(nc.sbuf_tensor("oag_sb", [128, 32 * 512], BF16))
            wo_db = e(nc.sbuf_tensor("wo_db", [128, NHT * 512], BF16))
            attn_sb = [e(nc.sbuf_tensor(f"attn{i}", [128, D], F32)) for i in range(4)]
            xo_sb = e(nc.sbuf_tensor("xo_sb", [128, D], F32))
            x1s_sb = e(nc.sbuf_tensor("x1s_sb", [128, D], BF16))
            sqd_sb = e(nc.sbuf_tensor("sqd_sb", [128, D], BF16))
            lnv = e(nc.sbuf_tensor("lnv", [128, 8], F32))
            pin = [e(nc.sbuf_tensor(f"pin{i}", [128, 2048], F32)) for i in range(2)]
            pi8 = [e(nc.sbuf_tensor(f"pi8{i}", [128, 2048], I8)) for i in range(2)]
            pbf = [e(nc.sbuf_tensor(f"pbf{i}", [128, 2048], BF16)) for i in range(2)]
            pev = [e(nc.sbuf_tensor(f"pev{i}", [128, 2048], BF16)) for i in range(2)]
            ps_c = [e(nc.psum_tensor(f"ps_c{i}", [128, 512], F32)) for i in range(4)]
            ps_tr = [e(nc.psum_tensor(f"ps_tr{i}", [128, 2048], BF16)) for i in range(2)]

            greg = {}

            def get_regs(g):
                if "base" not in greg:
                    greg["base"] = g.alloc_register("tok_base")
                    greg["tmp"] = g.alloc_register("tok_tmp")
                    g.wait_ge(sems["sl"], tb_ld)
                    g.reg_load(greg["base"], tb_sb[0:1, 0:1])
                return greg

            pst = dict(in_free=[0, 0], i8_free=[0, 0], bf_free=[0, 0], tr_free=[0, 0],
                       ev_free=[0, 0], idx=0)

            def plan_prep_tile(spec):
                i = pst["idx"]
                pst["idx"] += 1
                par = i % 2
                mat, blk, c0, ncols = spec
                srct = (wg, wu, wd)[mat]
                ld = plan.emit(
                    "sync",
                    lambda s, srct=srct, blk=blk, c0=c0, ncols=ncols, par=par:
                    s.dma_start(pin[par][:, 0:ncols],
                                srct[blk * 128:(blk + 1) * 128, c0:c0 + ncols]),
                    waits=(("sa", pst["in_free"][par]),), inc="sl", inc_by=16)
                q8 = plan.emit(
                    "scalar",
                    lambda sc, mat=mat, ncols=ncols, par=par:
                    sc.activation(pi8[par][:, 0:ncols], pin[par][:, 0:ncols], AF.Identity,
                                  scale=scl_sb[:, mat:mat + 1]),
                    waits=(("sl", ld), ("sv", pst["i8_free"][par]), ("sv", scl_ready)),
                    inc="sa")
                pst["in_free"][par] = q8
                cl = plan.emit(
                    "vector",
                    lambda v, ncols=ncols, par=par:
                    v.tensor_scalar(pbf[par][:, 0:ncols], pi8[par][:, 0:ncols],
                                    -1.0, 1.0, ALU.max, ALU.min),
                    waits=(("sa", q8), ("sp", pst["bf_free"][par])), inc="sv")
                pst["i8_free"][par] = cl
                nsub = ncols // 128
                tr = None
                for j in range(nsub):
                    tr = plan.emit(
                        "tensor",
                        lambda t, j=j, par=par:
                        t.transpose(ps_tr[par][:, j * 128:(j + 1) * 128],
                                    pbf[par][:, j * 128:(j + 1) * 128], id_sb[:]),
                        waits=(("sv", cl), ("sv", pst["tr_free"][par])), inc="sp")
                pst["bf_free"][par] = tr
                ev = plan.emit(
                    "vector",
                    lambda v, ncols=ncols, par=par:
                    v.tensor_copy(pev[par][:, 0:ncols], ps_tr[par][:, 0:ncols]),
                    waits=(("sp", tr), ("sg", pst["ev_free"][par])), inc="sv")
                pst["tr_free"][par] = ev
                if mat < 2:
                    dstt = (qg_t, qu_t)[mat]
                    ht0 = (c0 // 2048) * 16

                    def wr(g, dstt=dstt, it=blk, ht0=ht0, ncols=ncols, par=par):
                        dst = bass.AP(dstt, it * 128 * NHT * 128 + ht0 * 128,
                                      [[NHT * 128, 128], [1, ncols]])
                        return g.dma_start(dst, pev[par][:, 0:ncols])
                    wrv = plan.emit("gpsimd", wr, waits=(("sv", ev),), inc="sg", inc_by=16)
                else:
                    it_base = c0 // 128

                    def wr(g, ht=blk, it_base=it_base, ncols=ncols, par=par):
                        nsub = ncols // 128
                        dst = bass.AP(qd_t, it_base * 128 * D + ht * 128,
                                      [[D, 128], [128 * D, nsub], [1, 128]])
                        src = pev[par][:, 0:ncols].rearrange("p (j c) -> p j c", c=128)
                        return g.dma_start(dst, src)
                    wrv = plan.emit("gpsimd", wr, waits=(("sv", ev),), inc="sg", inc_by=16)
                pst["ev_free"][par] = wrv

            ln_apply = [0, 0, 0, 0]
            ln_sv = [0, 0, 0, 0]
            ln_done = 0
            x1s_free = 0
            prev_pe = 0
            for tb in range(4):
                oag_ready = 0
                for half_ in range(2):
                    def mkoag(g, half_=half_, tb=tb):
                        r = get_regs(g)
                        g.reg_add(r["tmp"], r["base"], half_ * 16 * 128 * S + tb * 512)
                        dst = oag_sb[:, half_ * 16 * 512:(half_ + 1) * 16 * 512].rearrange(
                            "p (h c) -> p h c", c=512)
                        return g.dma_start(dst, bass.AP(oav_ag, r["tmp"],
                                                        [[S, 128], [128 * S, 16], [1, 512]]))
                    oag_ready = plan.emit("gpsimd", mkoag,
                                          waits=(("scc", 1), ("sp", prev_pe)),
                                          inc="sg", inc_by=16)
                pe_db = prev_pe
                evac_done = [0, 0, 0, 0]
                for db in range(8):
                    wo_ready = plan.emit(
                        "sync",
                        lambda s, db=db: s.dma_start(
                            wo_db[:], bass.AP(wo_bf, db * 128 * NHT * 512,
                                              [[NHT * 512, 128], [1, NHT * 512]])),
                        waits=(("sp", pe_db), ("sg", wobf_done)),
                        inc="sl", inc_by=16)
                    for tt in range(4):
                        for hb in range(32):
                            wts = [("sg", oag_ready), ("sl", wo_ready)]
                            if hb == 0:
                                wts.append(("sv", evac_done[tt] or ln_sv[tt]))
                                wts.append(("sa", ln_apply[tt]))
                            mm = plan.emit(
                                "tensor",
                                lambda t, tt=tt, hb=hb:
                                t.matmul(ps_c[tt][:],
                                         oag_sb[:, hb * 512 + tt * 128: hb * 512 + (tt + 1) * 128],
                                         wo_db[:, hb * 512:(hb + 1) * 512],
                                         start=(hb == 0), stop=(hb == 31)),
                                waits=wts, inc="sp")
                        evac_done[tt] = plan.emit(
                            "vector",
                            lambda v, tt=tt, db=db:
                            v.tensor_copy(attn_sb[tt][:, db * 512:(db + 1) * 512], ps_c[tt][:]),
                            waits=(("sp", mm),), inc="sv")
                    pe_db = plan.count["sp"]
                prev_pe = pe_db
                for tt in range(4):
                    xo_ld = plan.emit(
                        "sync",
                        lambda s, tb=tb, tt=tt: s.dma_start(
                            xo_sb[:], x_own[(tb * 4 + tt) * 128:(tb * 4 + tt + 1) * 128, :]),
                        waits=(("sv", ln_done),), inc="sl", inc_by=16)
                    r1 = plan.emit(
                        "vector",
                        lambda v, tt=tt: v.tensor_tensor(attn_sb[tt][:], attn_sb[tt][:],
                                                         xo_sb[:], ALU.add),
                        waits=(("sl", xo_ld), ("sv", evac_done[tt])),
                        inc="sv")
                    dump_done = 0
                    if "attn_d" in dbg:
                        dump_done = plan.emit(
                            "gpsimd",
                            lambda g, tb=tb, tt=tt: g.dma_start(
                                attn_d.ap()[(tb * 4 + tt) * 128:(tb * 4 + tt + 1) * 128, :],
                                attn_sb[tt][:]),
                            waits=(("sv", r1),), inc="sg", inc_by=16)
                    r2 = plan.emit(
                        "vector",
                        lambda v, tt=tt: v.reduce_sum(lnv[:, 0:1], attn_sb[tt][:], AX.X),
                        waits=(("sv", r1),), inc="sv")
                    r3a = plan.emit(
                        "vector",
                        lambda v, tt=tt: v.tensor_tensor(sqd_sb[:], attn_sb[tt][:],
                                                         attn_sb[tt][:], ALU.mult),
                        waits=(("sv", r2),), inc="sv")
                    r3 = plan.emit(
                        "vector",
                        lambda v: v.reduce_sum(lnv[:, 1:2], sqd_sb[:], AX.X),
                        waits=(("sv", r3a),), inc="sv")
                    r4 = plan.emit(
                        "vector",
                        lambda v: v.tensor_scalar(lnv[:, 2:3], lnv[:, 0:1], 1.0 / D, None, ALU.mult),
                        waits=(("sv", r2),), inc="sv")
                    r5 = plan.emit(
                        "vector",
                        lambda v: v.tensor_scalar(lnv[:, 3:4], lnv[:, 1:2], 1.0 / D, None, ALU.mult),
                        waits=(("sv", r3), ("sv", r4)), inc="sv")
                    r6 = plan.emit(
                        "vector",
                        lambda v: v.tensor_tensor(lnv[:, 6:7], lnv[:, 2:3], lnv[:, 2:3], ALU.mult),
                        waits=(("sv", r5),), inc="sv")
                    r7 = plan.emit(
                        "vector",
                        lambda v: v.tensor_sub(lnv[:, 3:4], lnv[:, 3:4], lnv[:, 6:7]),
                        waits=(("sv", r6),), inc="sv")
                    r7b = plan.emit(
                        "vector",
                        lambda v: v.tensor_scalar(lnv[:, 3:4], lnv[:, 3:4], LNEPS, None, ALU.add),
                        waits=(("sv", r7),), inc="sv")
                    r8 = plan.emit(
                        "scalar",
                        lambda sc: sc.activation(lnv[:, 7:8], lnv[:, 3:4], AF.Ln),
                        waits=(("sv", r7b),), inc="sa")
                    r9 = plan.emit(
                        "scalar",
                        lambda sc: sc.activation(lnv[:, 4:5], lnv[:, 7:8], AF.Exp, scale=-0.5),
                        waits=(("sa", r8),), inc="sa")
                    r10 = plan.emit(
                        "vector",
                        lambda v: v.tensor_tensor(lnv[:, 5:6], lnv[:, 2:3], lnv[:, 4:5], ALU.mult),
                        waits=(("sa", r9),), inc="sv")
                    r11 = plan.emit(
                        "vector",
                        lambda v: v.tensor_scalar(lnv[:, 5:6], lnv[:, 5:6], -1.0, None, ALU.mult),
                        waits=(("sv", r10),), inc="sv")
                    if "lnv_d" in dbg:
                        plan.emit(
                            "gpsimd",
                            lambda g, tb=tb, tt=tt: g.dma_start(
                                lnv_d.ap()[(tb * 4 + tt) * 128:(tb * 4 + tt + 1) * 128, :],
                                lnv[:]),
                            waits=(("sv", r11), ("sa", r8)), inc="sg", inc_by=16)
                    ap_ = plan.emit(
                        "scalar",
                        lambda sc, tt=tt: sc.activation(x1s_sb[:], attn_sb[tt][:], AF.Identity,
                                                        scale=lnv[:, 4:5], bias=lnv[:, 5:6]),
                        waits=(("sv", r11), ("sg", x1s_free)), inc="sa")
                    x1s_free = plan.emit(
                        "gpsimd",
                        lambda g, tb=tb, tt=tt: g.dma_start(
                            x1d.ap()[(tb * 4 + tt) * 128:(tb * 4 + tt + 1) * 128, :], x1s_sb[:]),
                        waits=(("sa", ap_),), inc="sg", inc_by=16)
                    ln_done = r11
                    ln_apply[tt] = ap_
                    ln_sv[tt] = r11
                for spec in prep_slices[tb]:
                    plan_prep_tile(spec)

            plan.barrier()
            plan.drain(block, sems)

        if stop_after == "C":
            _finish_debug(nc, plan, block, sems, dbg, scratch)
            return nc

        dn_evac_prev = [[0] * 4 for _ in range(8)]
        out_free = 0
        for tb in range(4):
            with ExitStack() as ftb:
                ee = ftb.enter_context
                inter = ee(nc.sbuf_tensor(f"inter{tb}", [128, NIT * 512], BF16))
                with ExitStack() as f1:
                    e = f1.enter_context
                    x1T = e(nc.sbuf_tensor(f"x1T_{tb}", [128, NHT * 512], BF16))
                    qgs = [e(nc.sbuf_tensor(f"qgs{tb}_{i}", [128, NHT * 128], BF16))
                           for i in range(2)]
                    qus = [e(nc.sbuf_tensor(f"qus{tb}_{i}", [128, NHT * 128], BF16))
                           for i in range(2)]
                    gts = e(nc.sbuf_tensor(f"gts{tb}", [128, 512], BF16))
                    ps_g = e(nc.psum_tensor(f"ps_g{tb}", [128, 512], F32))
                    ps_u = e(nc.psum_tensor(f"ps_u{tb}", [128, 512], F32))

                    sv_entry = plan.count["sv"]
                    sa_entry = plan.count["sa"]
                    sout_entry = plan.count["sout"]
                    x1t_ready = 0
                    for d in range(NHT):
                        x1t_ready = plan.emit(
                            "sync",
                            lambda s, d=d, tb=tb: s.dma_start(
                                x1T[:, d * 512:(d + 1) * 512],
                                x1d.ap()[tb * 512:(tb + 1) * 512, d * 128:(d + 1) * 128],
                                transpose=True),
                            waits=(("sg", plan.count["sg"]), ("sv", sv_entry),
                                   ("sa", sa_entry), ("sout", sout_entry)),
                            inc="sl", inc_by=16)

                    qg_ready = [0, 0]
                    qu_ready = [0, 0]
                    qgbuf_free = [0, 0]
                    g_evac = 0
                    u_cons = 0
                    mmg = mmu = 0
                    for it in range(NIT):
                        par = it % 2
                        qg_ready[par] = plan.emit(
                            "sync",
                            lambda s, it=it, par=par: s.dma_start(
                                qgs[par][:], bass.AP(qg_t, it * 128 * NHT * 128,
                                                     [[NHT * 128, 128], [1, NHT * 128]])),
                            waits=(("sp", qgbuf_free[par]),), inc="sl", inc_by=16)
                        qu_ready[par] = plan.emit(
                            "sync",
                            lambda s, it=it, par=par: s.dma_start(
                                qus[par][:], bass.AP(qu_t, it * 128 * NHT * 128,
                                                     [[NHT * 128, 128], [1, NHT * 128]])),
                            inc="sl", inc_by=16)
                        for d in range(NHT):
                            wts = [("sl", qg_ready[par]), ("sl", x1t_ready)]
                            if d == 0:
                                wts.append(("sa", g_evac))
                                wts.append(("sv", sv_entry))
                            mmg = plan.emit(
                                "tensor",
                                lambda t, d=d, par=par:
                                t.matmul(ps_g[:], qgs[par][:, d * 128:(d + 1) * 128],
                                         x1T[:, d * 512:(d + 1) * 512],
                                         start=(d == 0), stop=(d == NHT - 1)),
                                waits=wts, inc="sp")
                        g_evac = plan.emit(
                            "scalar", lambda sc: sc.activation(gts[:], ps_g[:], AF.Copy),
                            waits=(("sp", mmg), ("sv", u_cons)), inc="sa")
                        for d in range(NHT):
                            wts = [("sl", qu_ready[par])]
                            if d == 0:
                                wts.append(("sv", u_cons))
                            mmu = plan.emit(
                                "tensor",
                                lambda t, d=d, par=par:
                                t.matmul(ps_u[:], qus[par][:, d * 128:(d + 1) * 128],
                                         x1T[:, d * 512:(d + 1) * 512],
                                         start=(d == 0), stop=(d == NHT - 1)),
                                waits=wts, inc="sp")
                        qgbuf_free[par] = mmu
                        u_cons = plan.emit(
                            "vector",
                            lambda v, it=it: v.tensor_tensor(inter[:, it * 512:(it + 1) * 512],
                                                             ps_u[:], gts[:], ALU.mult),
                            waits=(("sp", mmu), ("sa", g_evac)), inc="sv")
                    inter_done = u_cons
                    gateup_pe = plan.count["sp"]
                    gateup_sa = plan.count["sa"]
                    plan.drain(block, sems)

                with ExitStack() as f2:
                    e = f2.enter_context
                    qds = [e(nc.sbuf_tensor(f"qds{tb}_{i}", [128, 8 * 512], BF16))
                           for i in range(2)]
                    dn_sb = [e(nc.sbuf_tensor(f"dn{tb}_{i}", [128, D], F32)) for i in range(4)]
                    x1r = e(nc.sbuf_tensor(f"x1r{tb}", [128, D], BF16))
                    res2 = e(nc.sbuf_tensor(f"res2{tb}", [128, D], F32))
                    outs = e(nc.sbuf_tensor(f"outs{tb}", [128, D], F32))
                    lnv2 = e(nc.sbuf_tensor(f"lnv2{tb}", [128, 8], F32))
                    ps_d = [e(nc.psum_tensor(f"ps_d{tb}_{i}", [128, 512], F32))
                            for i in range(4)]

                    dn_evac = [[0] * 4 for _ in range(8)]
                    qd_grp_free = [0, 0]
                    for db in range(8):
                        for itg in range(11):
                            par = itg % 2
                            i0 = itg * 8
                            i1 = min(NIT, i0 + 8)
                            wts = [("sp", max(qd_grp_free[par], gateup_pe)),
                                   ("sa", gateup_sa)]

                            def mkqd(s, i0=i0, i1=i1, db=db, par=par):
                                src = bass.AP(qd_t, i0 * 128 * D + db * 512,
                                              [[D, 128], [128 * D, i1 - i0], [1, 512]])
                                dst = qds[par][:, 0:(i1 - i0) * 512].rearrange(
                                    "p (j c) -> p j c", c=512)
                                return s.dma_start(dst, src)
                            qd_ready = plan.emit("sync", mkqd, waits=wts, inc="sl", inc_by=16)
                            for it in range(i0, i1):
                                for tt in range(4):
                                    wts2 = [("sl", qd_ready), ("sv", inter_done)]
                                    if it == 0:
                                        wts2.append(("sv", max(dn_evac_prev[db][tt],
                                                               dn_evac[db - 1][tt] if db else 0)))
                                    mmd = plan.emit(
                                        "tensor",
                                        lambda t, it=it, i0=i0, tt=tt, par=par:
                                        t.matmul(ps_d[tt][:],
                                                 inter[:, it * 512 + tt * 128:
                                                       it * 512 + (tt + 1) * 128],
                                                 qds[par][:, (it - i0) * 512:(it - i0 + 1) * 512],
                                                 start=(it == 0), stop=(it == NIT - 1)),
                                        waits=wts2, inc="sp")
                            qd_grp_free[par] = plan.count["sp"]
                        pe_db = plan.count["sp"]
                        for tt in range(4):
                            dn_evac[db][tt] = plan.emit(
                                "vector",
                                lambda v, tt=tt, db=db:
                                v.tensor_copy(dn_sb[tt][:, db * 512:(db + 1) * 512], ps_d[tt][:]),
                                waits=(("sp", pe_db),), inc="sv")
                    dn_evac_prev = dn_evac

                    ln2_done = 0
                    apo = 0
                    for tt in range(4):
                        x1_ld = plan.emit(
                            "sync",
                            lambda s, tb=tb, tt=tt: s.dma_start(
                                x1r[:], x1d.ap()[(tb * 4 + tt) * 128:(tb * 4 + tt + 1) * 128, :]),
                            waits=(("sv", ln2_done),), inc="sl", inc_by=16)
                        q1 = plan.emit(
                            "vector",
                            lambda v, tt=tt: v.scalar_tensor_tensor(
                                res2[:], dn_sb[tt][:], scl_sb[:, 3:4], x1r[:],
                                ALU.mult, ALU.add),
                            waits=[("sl", x1_ld), ("sv", ln2_done), ("sa", apo)] +
                                  [("sv", dn_evac[dbq][tt]) for dbq in range(8)],
                            inc="sv")
                        q2 = plan.emit(
                            "vector", lambda v: v.reduce_sum(lnv2[:, 0:1], res2[:], AX.X),
                            waits=(("sv", q1),), inc="sv")
                        q3a = plan.emit(
                            "vector",
                            lambda v: v.tensor_tensor(x1r[:], res2[:], res2[:], ALU.mult),
                            waits=(("sv", q2),), inc="sv")
                        q3 = plan.emit(
                            "vector",
                            lambda v: v.reduce_sum(lnv2[:, 1:2], x1r[:], AX.X),
                            waits=(("sv", q3a),), inc="sv")
                        q4 = plan.emit(
                            "vector",
                            lambda v: v.tensor_scalar(lnv2[:, 2:3], lnv2[:, 0:1],
                                                      1.0 / D, None, ALU.mult),
                            waits=(("sv", q2),), inc="sv")
                        q5 = plan.emit(
                            "vector",
                            lambda v: v.tensor_scalar(lnv2[:, 3:4], lnv2[:, 1:2],
                                                      1.0 / D, None, ALU.mult),
                            waits=(("sv", q3), ("sv", q4)), inc="sv")
                        q6 = plan.emit(
                            "vector",
                            lambda v: v.tensor_tensor(lnv2[:, 6:7], lnv2[:, 2:3],
                                                      lnv2[:, 2:3], ALU.mult),
                            waits=(("sv", q5),), inc="sv")
                        q7 = plan.emit(
                            "vector",
                            lambda v: v.tensor_sub(lnv2[:, 3:4], lnv2[:, 3:4], lnv2[:, 6:7]),
                            waits=(("sv", q6),), inc="sv")
                        q7b = plan.emit(
                            "vector",
                            lambda v: v.tensor_scalar(lnv2[:, 3:4], lnv2[:, 3:4],
                                                      LNEPS, None, ALU.add),
                            waits=(("sv", q7),), inc="sv")
                        q8 = plan.emit(
                            "scalar",
                            lambda sc: sc.activation(lnv2[:, 7:8], lnv2[:, 3:4], AF.Ln),
                            waits=(("sv", q7b),), inc="sa")
                        q9 = plan.emit(
                            "scalar",
                            lambda sc: sc.activation(lnv2[:, 4:5], lnv2[:, 7:8], AF.Exp,
                                                     scale=-0.5),
                            waits=(("sa", q8),), inc="sa")
                        q10 = plan.emit(
                            "vector",
                            lambda v: v.tensor_tensor(lnv2[:, 5:6], lnv2[:, 2:3],
                                                      lnv2[:, 4:5], ALU.mult),
                            waits=(("sa", q9),), inc="sv")
                        q11 = plan.emit(
                            "vector",
                            lambda v: v.tensor_scalar(lnv2[:, 5:6], lnv2[:, 5:6],
                                                      -1.0, None, ALU.mult),
                            waits=(("sv", q10),), inc="sv")
                        apo = plan.emit(
                            "scalar",
                            lambda sc: sc.activation(outs[:], res2[:], AF.Identity,
                                                     scale=lnv2[:, 4:5], bias=lnv2[:, 5:6]),
                            waits=(("sv", q11), ("sout", out_free)), inc="sa")
                        out_free = plan.emit(
                            "gpsimd",
                            lambda g, tb=tb, tt=tt: g.dma_start(
                                out[(tb * 4 + tt) * 128:(tb * 4 + tt + 1) * 128, :], outs[:]),
                            waits=(("sa", apo),), inc="sout", inc_by=16)
                        ln2_done = q11
                    plan.drain(block, sems)

        plan.emit("gpsimd", None, waits=(("sout", plan.count["sout"]),))
        plan.drain(block, sems)

    return nc


def _finish_debug(nc, plan, block, sems, dbg, scratch):
    for name, o in dbg.items():
        t = scratch[name]
        plan.emit("gpsimd", lambda g, t=t, o=o: g.dma_start(o, t.ap().opt()),
                  inc="sg", inc_by=16)
    plan.emit("gpsimd", None, waits=(("sg", plan.count["sg"]),))
    plan.drain(block, sems)


def make_in_maps(x, wq, wk, wv, wo, wg, wu, wd):
    mask = np.zeros((128, 128), dtype=np.float32)
    for k in range(128):
        mask[k, :k] = NEG
    ident = np.eye(128, dtype=ml_dtypes.bfloat16)
    in_maps = []
    for c in range(8):
        b, half = c // 2, c % 2
        in_maps.append({
            "x_full": np.ascontiguousarray(x[b]),
            "x_own": np.ascontiguousarray(x[b, half * TOK:(half + 1) * TOK]),
            "wq": np.ascontiguousarray(wq[:, half * 2048:(half + 1) * 2048]),
            "wk": np.ascontiguousarray(wk[:, half * 512:(half + 1) * 512]),
            "wv": np.ascontiguousarray(wv[:, half * 512:(half + 1) * 512]),
            "wo": wo,
            "wg": wg, "wu": wu, "wd": wd,
            "tokbase": np.array([[half * TOK]], dtype=np.int32),
            "mask_tri": mask,
            "identity": ident,
        })
    return in_maps


def kernel(L0_attn_x, L0_attn_wq, L0_attn_wk, L0_attn_wv, L0_attn_wo,
           L0_ffn_w_gate, L0_ffn_w_up, L0_ffn_w_down):
    x = np.ascontiguousarray(np.asarray(L0_attn_x, dtype=np.float32))
    wq = np.asarray(L0_attn_wq, dtype=np.float32)
    wk = np.asarray(L0_attn_wk, dtype=np.float32)
    wv = np.asarray(L0_attn_wv, dtype=np.float32)
    wo = np.ascontiguousarray(np.asarray(L0_attn_wo, dtype=np.float32))
    wg = np.ascontiguousarray(np.asarray(L0_ffn_w_gate, dtype=np.float32))
    wu = np.ascontiguousarray(np.asarray(L0_ffn_w_up, dtype=np.float32))
    wd = np.ascontiguousarray(np.asarray(L0_ffn_w_down, dtype=np.float32))

    nc = build_program()
    in_maps = make_in_maps(x, wq, wk, wv, wo, wg, wu, wd)
    res = run_bass_kernel_spmd(nc, in_maps, core_ids=list(range(8)))
    outv = np.empty((4, S, D), dtype=np.float32)
    for c in range(8):
        b, half = c // 2, c % 2
        outv[b, half * TOK:(half + 1) * TOK] = res.results[c]["out"]
    return outv
